# version 03 (final)
# baseline (speedup 1.0000x reference)
import contextlib
import numpy as np
import ml_dtypes
import concourse.bass as bass
import concourse.mybir as mybir
from concourse.bass_utils import run_bass_kernel_spmd

F32 = mybir.dt.float32
BF16 = mybir.dt.bfloat16
AF = mybir.ActivationFunctionType
ALU = mybir.AluOpType
AXX = mybir.AxisListType.X

NCORE = 8
TPC = 2048
HALO = 4
TH = TPC + HALO
H = 2048
DS = 4096
PD = 8512
NH = 64
CS = 256
NCH = TPC // CS
NE = 16
FI = 1408
FIS = 2816
KH = H // 128
KDS = DS // 128
EPS = 1e-5

ENGS = ("sync", "scalar", "vector", "tensor", "gpsimd")


class SemT:
    def __init__(self, nc, es, name):
        self.h = es.enter_context(nc.semaphore(name))
        self.n = 0

    def bump(self, k=1):
        self.n += k
        return self.n


class Prog:

    def __init__(self, nc, lastwait, selfsems, lastdone):
        self.nc = nc
        self.q = {e: [] for e in ENGS}
        self.lastwait = lastwait
        self.selfsems = selfsems
        self.lastdone = lastdone

    def step(self, eng, fn, inc=None, k=1):
        ld = self.lastdone.get(eng)
        if ld is not None:
            self.wait(eng, ld[0], ld[1])
        if inc is None:
            inc = self.selfsems[eng]
            k = 1
        v = inc.bump(k)
        self.q[eng].append(lambda e, f=fn, s=inc.h, kk=k: f(e).then_inc(s, kk))
        self.lastdone[eng] = (inc, v)
        return v

    def wait(self, eng, sem, val):
        if val is None or val <= 0:
            return
        key = id(sem)
        if val <= self.lastwait[eng].get(key, 0):
            return
        self.lastwait[eng][key] = val
        self.q[eng].append(lambda e, s=sem.h, v=val: e.wait_ge(s, v))

    def waitt(self, eng, sv):
        if sv is not None:
            self.wait(eng, sv[0], sv[1])

    def run(self, name):
        with self.nc.Block(name) as blk:
            for eng in ENGS:
                fns = self.q[eng]
                if fns:
                    getattr(blk, eng)(lambda e, fs=tuple(fns): [f(e) for f in fs])


class Guards:
    def __init__(self, nc):
        self.nc = nc
        self.gs = []

    def sb(self, name, shape, dt, side=None):
        if side is None:
            g = self.nc.sbuf_tensor(name, list(shape), dt)
        else:
            g = self.nc.sbuf_tensor(name, list(shape), dt, side=side)
        t = g.__enter__()
        self.gs.append(g)
        return t

    def close(self):
        for g in reversed(self.gs):
            g.__exit__(None, None, None)
        self.gs = []


def build_nc(dbg=False):
    nc = bass.Bass("TRN2", target_bir_lowering=False)
    es = contextlib.ExitStack()
    scratch_kind = "ExternalOutput" if dbg else "Internal"

    def din(name, shape):
        return nc.dram_tensor(name, list(shape), F32, kind="ExternalInput").ap()

    xin = din("xin", (TH, H))
    WinT = din("WinT", (H, PD))
    WoutT = din("WoutT", (DS, H))
    gateTd = din("gateT", (H, NE))
    w13Td = din("w13T", (NE * H, 2 * FI))
    w2Td = din("w2T", (NE * FI, H))
    sw13Td = din("sw13T", (H, 2 * FIS))
    sw2Td = din("sw2T", (FIS, H))
    convwD = din("convw", (128, 34 * 4))
    convbD = din("convb", (128, 34))
    dtbD = din("dtb", (NH, 1))
    alogD = din("alog", (NH, 1))
    dpairD = din("dpair", (2, 32))
    rmswD = din("rmsw", (128, KH))
    normwD = din("normw", (128, KDS))
    cIdent = din("cIdent", (128, 128))
    cSel2F = din("cSel2F", (2, 128))
    cSel16F = din("cSel16F", (16, 16 * 128))
    cSelPair = din("cSelPair", (64, 32 * 128))
    cE64 = din("cE64", (64, DS))
    cOnes64F = din("cOnes64F", (64, 128))
    cOnesCol = din("cOnesCol", (128, 1))
    cTril = din("cTril", (128, 384))
    cMaskB = din("cMaskB", (128, 4))
    cMaskQD = din("cMaskQ", (128, 4))
    cTselD = din("cTsel", (64, 4))
    cTselCD = din("cTselC", (64, 4))

    out = nc.dram_tensor("out", [TPC, H], F32, kind="ExternalOutput").ap()

    def dsc(name, shape, dt=F32):
        return nc.dram_tensor(name, list(shape), dt, kind=scratch_kind).ap()

    dXT = dsc("dXT", (128, KH * TPC), BF16)
    dXS = dsc("dXS", (DS, TPC), BF16)
    dZ = dsc("dZ", (DS, TPC), BF16)
    dST = dsc("dST", (NCH * 128, DS), F32)
    dY = dsc("dY", (DS, TPC), BF16)
    dG = dsc("dG", (TPC, H), F32)
    dPR = dsc("dPR", (512, 2048), F32)
    dRS = dsc("dRS", (1, TPC), F32)
    dAGi = nc.dram_tensor("dAGi", [4 * 128, DS + 1], F32, kind="Internal").ap()
    dAGo = nc.dram_tensor("dAGo", [4 * 128, DS + 1], F32, kind="Internal").ap()

    def sb(name, shape, dt):
        return es.enter_context(nc.sbuf_tensor(name, list(shape), dt))

    lnm = sb("lnm", (128, 16), F32)
    lnr = sb("lnr", (128, 16), F32)
    rstdT = sb("rstdT", (128, 16), F32)
    acol = sb("acol", (NH, 1), F32)
    dcol32 = sb("dcol32", (128, 32), F32)
    idb = sb("idb", (128, 128), BF16)
    idf = sb("idf", (128, 128), F32)
    onecolB = sb("onecolB", (128, 1), BF16)
    dtbS = sb("dtbS", (NH, 1), F32)
    epscol = sb("epscol", (128, 1), F32)
    onecolF = sb("onecolF", (128, 1), F32)

    psF = es.enter_context(nc.psum_tensor("psF", [128, 3072], F32))
    psB = es.enter_context(nc.psum_tensor("psB", [128, 2048], BF16))

    def S(i, n=512, p=128):
        return psF[0:p, i * 512:i * 512 + n]

    def Sv(c0, n, p=128):
        return psF[0:p, c0:c0 + n]

    def SBk(i, n=1024, p=128):
        return psB[0:p, i * 1024:i * 1024 + n]

    sq = SemT(nc, es, "sqS")
    sa = SemT(nc, es, "sqA")
    sg = SemT(nc, es, "sqG")
    sPE = SemT(nc, es, "sPE")
    sDV = SemT(nc, es, "sDV")
    sAC = SemT(nc, es, "sAC")
    sPO = SemT(nc, es, "sPO")
    sCC = SemT(nc, es, "sCC")
    sOut = SemT(nc, es, "sOut")
    selfsems = {e: SemT(nc, es, "slf_" + e) for e in ENGS}
    ALLSEMS = (sq, sa, sg, sPE, sDV, sAC, sPO, sCC, sOut) + tuple(
        selfsems[e] for e in ENGS)

    lastwait = {e: {} for e in ENGS}
    lastdone = {}
    psb_slot = [None, None]
    slot_cons = [None] * 6

    def barrier(p):
        for eng in ENGS:
            for s in ALLSEMS:
                p.wait(eng, s, s.n)

    g0 = Guards(nc)
    alogS = g0.sb("alogS", (NH, 1), F32)
    dpairS = g0.sb("dpairS", (2, 32), F32)
    sel2S = g0.sb("sel2S", (2, 128), F32)

    p = Prog(nc, lastwait, selfsems, lastdone)
    p.step("gpsimd", lambda e: e.dma_start(out=idb[:, :], in_=cIdent), sg, 16)
    p.step("gpsimd", lambda e: e.dma_start(out=onecolB[:, :], in_=cOnesCol), sg, 16)
    p.step("sync", lambda e: e.dma_start(out=idf[:, :], in_=cIdent), sq, 16)
    p.step("sync", lambda e: e.dma_start(out=dtbS[:, :], in_=dtbD), sq, 16)
    p.step("sync", lambda e: e.dma_start(out=alogS[:, :], in_=alogD), sq, 16)
    p.step("sync", lambda e: e.dma_start(out=dpairS[:, :], in_=dpairD), sq, 16)
    p.step("sync", lambda e: e.dma_start(out=sel2S[:, :], in_=cSel2F), sq, 16)
    p.step("vector", lambda e: e.memset(epscol[:, :], EPS))
    p.step("vector", lambda e: e.memset(onecolF[:, :], 1.0))
    p.wait("scalar", sq, sq.n)
    p.step("scalar", lambda e: e.activation(acol[:, :], alogS[:, :], AF.Exp), sAC)
    p.wait("vector", sAC, sAC.n)
    p.step("vector", lambda e: e.tensor_scalar_mul(acol[:, :], acol[:, :], -1.0), sDV)
    p.wait("tensor", sq, sq.n)
    vpe = p.step("tensor", lambda e: e.matmul(S(0, 32), sel2S[:, :], dpairS[:, :],
                                              start=True, stop=True), sPE)
    p.wait("vector", sPE, vpe)
    p.step("vector", lambda e: e.tensor_copy(dcol32[:, :], S(0, 32)), sDV)
    p.run("ph0")
    g0.close()

    gA = Guards(nc)
    Bres = gA.sb("Bres", (128, TPC), BF16, side="right")
    Cres = gA.sb("Cres", (128, TPC), BF16, side="right")
    dtFv = gA.sb("dtF", (NH, TPC), F32, side="right")
    dAcF = gA.sb("dAcF", (NH, TPC), F32, side="right")
    dAcR = gA.sb("dAcR", (NH, TPC), F32, side="right")
    rows2 = gA.sb("rows2", (NH, TPC), F32, side="right")
    expdAcB = gA.sb("expdAcB", (NH, TPC), BF16, side="right")
    cdsB = gA.sb("cdsB", (NH, TPC), BF16, side="right")
    logdp = gA.sb("logdp", (NH, 1), F32, side="right")
    carry = gA.sb("carry", (128, DS), F32, side="right")

    gH = Guards(nc)
    HT = gH.sb("HT", (128, KH * TH), BF16)

    g1 = Guards(nc)
    xstg = [g1.sb("xstg0", (128, H), F32)]
    xcb = g1.sb("xcb", (128, H), F32)
    xnb = g1.sb("xnb", (128, H), BF16)
    hbb = g1.sb("hbb", (128, H), BF16)
    xtst = [g1.sb("xtst0", (128, KH * 128), BF16)]
    rmswS = g1.sb("rmswS", (128, KH), F32)
    c_ss = g1.sb("c_ss", (128, 1), F32)
    c_t = g1.sb("c_t", (128, 1), F32)
    c_hf = g1.sb("c_hf", (128, 1), F32)
    c_rh = g1.sb("c_rh", (128, 1), F32)

    p = Prog(nc, lastwait, selfsems, lastdone)
    barrier(p)
    p.step("sync", lambda e: e.dma_start(out=rmswS[:, :], in_=rmswD), sq, 16)
    rms_ld = sq.n
    xfree = [0]
    xtfree = [0]

    def ln_block(row0, n, negm_ap, rstd_ap, buf):
        p.wait("sync", sAC, xfree[0])
        vld = p.step("sync", lambda e, b=buf, r=row0, nn=n: e.dma_start(
            out=xstg[0][0:nn, :], in_=xin[r:r + nn, :]), sq, 16)
        p.wait("vector", sq, vld)
        p.step("vector", lambda e, nn=n, na=negm_ap: e.reduce_sum(
            na, xstg[0][0:nn, :], axis=AXX, negate=True))
        p.step("vector", lambda e, na=negm_ap: e.tensor_scalar_mul(na, na, 1.0 / H))
        vxc = p.step("vector", lambda e, nn=n, na=negm_ap: e.tensor_scalar_add(
            xcb[0:nn, :], xstg[0][0:nn, :], na), sDV)
        p.wait("scalar", sDV, vxc)
        vsq = p.step("scalar", lambda e, nn=n: e.activation(
            xstg[0][0:nn, :], xcb[0:nn, :], AF.Square, accum_out=c_ss[0:nn, :]), sAC)
        xfree[0] = vsq
        p.step("scalar", lambda e, nn=n: e.activation(
            c_t[0:nn, :], c_ss[0:nn, :], AF.Ln, bias=epscol[0:nn, :], scale=1.0 / H))
        vr = p.step("scalar", lambda e, nn=n, ra=rstd_ap: e.activation(
            ra, c_t[0:nn, :], AF.Exp, scale=-0.5), sAC)
        p.wait("vector", sAC, vr)
        vt = p.step("vector", lambda e, nn=n, ra=rstd_ap: e.tensor_scalar(
            c_t[0:nn, :], c_ss[0:nn, :], ra, ra, ALU.mult, ALU.mult), sDV)
        p.wait("scalar", sDV, vt)
        p.step("scalar", lambda e, nn=n: e.activation(
            c_t[0:nn, :], c_t[0:nn, :], AF.Ln, bias=epscol[0:nn, :], scale=1.0 / H))
        v2 = p.step("scalar", lambda e, nn=n: e.activation(
            c_t[0:nn, :], c_t[0:nn, :], AF.Exp, scale=-0.5), sAC)
        p.wait("vector", sAC, v2)
        p.step("vector", lambda e, nn=n, ra=rstd_ap: e.tensor_scalar(
            c_hf[0:nn, :], c_t[0:nn, :], ra, None, ALU.mult))

    def do_transposes(src, batches, evac_fn):
        for bi in batches:
            kbase = bi * 8
            bank = bi % 2
            p.waitt("tensor", psb_slot[bank])
            vtp = None
            for j in range(8):
                kc = kbase + j
                fn = (lambda e, s=src, k=kc, b=bank, jj=j: e.transpose(
                    SBk(b)[:, jj * 128:jj * 128 + 128], s[:, k * 128:k * 128 + 128],
                    idb[:, :]))
                if j == 7:
                    vtp = p.step("tensor", fn, sPE)
                else:
                    p.step("tensor", fn)
            p.wait("vector", sPE, vtp)
            vev = evac_fn(kbase, bank)
            psb_slot[bank] = (sDV, vev)

    for tt in range(16):
        buf = tt % 2
        ln_block(HALO + tt * 128, 128, lnm[:, tt:tt + 1], lnr[:, tt:tt + 1], buf)
        p.step("vector", lambda e, t=tt: e.tensor_scalar(
            xnb[:, :], xcb[:, :], lnr[:, t:t + 1], None, ALU.mult))
        vhb = p.step("vector", lambda e: e.tensor_scalar(
            hbb[:, :], xcb[:, :], c_hf[:, :], None, ALU.mult), sDV)
        p.wait("tensor", sDV, vhb)
        p.wait("vector", sq, rms_ld)

        def evac_xt(kbase, bank, t=tt):
            return p.step("vector", lambda e, b=bank, kb=kbase: e.tensor_copy(
                xtst[0][:, kb * 128:kb * 128 + 1024], SBk(b)), sDV)

        def evac_ht(kbase, bank, t=tt):
            vev = None
            for j in range(8):
                kc = kbase + j
                vev = p.step("vector", lambda e, b=bank, k=kc, jj=j: e.tensor_scalar(
                    HT[:, k * TH + HALO + t * 128: k * TH + HALO + t * 128 + 128],
                    SBk(b)[:, jj * 128:jj * 128 + 128],
                    rmswS[:, k:k + 1], None, ALU.mult), sDV)
            return vev

        p.waitt("vector", (sa, xtfree[0]))
        do_transposes(xnb, [0, 1], evac_xt)
        xt_ready = psb_slot[1][1]
        do_transposes(hbb, [0, 1], evac_ht)
        p.wait("scalar", sDV, xt_ready)
        xtfree[0] = p.step("scalar", lambda e, t=tt: e.dma_start(
            out=dXT.rearrange("p (kc t) -> p kc t", t=TPC)[:, :, t * 128:(t + 1) * 128],
            in_=xtst[0][:, :].rearrange("p (kc t) -> p kc t", t=128)), sa, 16)

    ln_block(0, HALO, c_rh[0:HALO, 0:1], c_rh[0:HALO, 0:1], 0)
    vhb = p.step("vector", lambda e: e.tensor_scalar(
        hbb[0:HALO, :], xcb[0:HALO, :], c_hf[0:HALO, :], None, ALU.mult), sDV)
    p.wait("tensor", sDV, vhb)

    def evac_ht_halo(kbase, bank):
        vev = None
        for j in range(8):
            kc = kbase + j
            vev = p.step("vector", lambda e, b=bank, k=kc, jj=j: e.tensor_scalar(
                HT[:, k * TH: k * TH + HALO],
                SBk(b)[:, jj * 128:jj * 128 + HALO],
                rmswS[:, k:k + 1], None, ALU.mult), sDV)
        return vev

    do_transposes(hbb, [0, 1], evac_ht_halo)
    p.run("ph1")
    g1.close()

    g2 = Guards(nc)
    wslab = [g2.sb(f"wslab{i}", (128, KH * 256), BF16) for i in range(2)]
    xbcf = [g2.sb(f"xbcf{i}", (128, TH), F32) for i in range(2)]
    convacc = [g2.sb(f"convacc{i}", (128, TPC), F32) for i in range(2)]
    zstg = [g2.sb(f"zstg{i}", (128, 512), BF16) for i in range(2)]
    xsstg = [g2.sb(f"xsstg{i}", (128, TPC), BF16) for i in range(2)]
    dtA = g2.sb("dtA", (NH, TPC), F32)
    t1b = convacc[1][0:NH, :]
    t2b = convacc[0][0:NH, :]
    convwS = g2.sb("convwS", (128, 34 * 4), F32)
    convbS = g2.sb("convbS", (128, 34), F32)

    p = Prog(nc, lastwait, selfsems, lastdone)
    barrier(p)
    p.step("sync", lambda e: e.dma_start(out=convwS[:, :], in_=convwD), sq, 16)
    p.step("sync", lambda e: e.dma_start(out=convbS[:, :], in_=convbD), sq, 16)
    cw_ld = sq.n

    slabs = [(i * 256, 256, 'z') for i in range(16)] + \
            [(DS + i * 256, 256, 'x') for i in range(17)] + \
            [(2 * DS + 256, 64, 'dt')]

    slab_ld = [0, 0]
    slab_pe = [0, 0]
    zfree = [0, 0]
    xsfree = [0, 0]
    xbc_cons = [0, 0]
    cacc_cons = [0, 0]

    def load_slab(si, b):
        f0, fw, _ = slabs[si]
        p.wait("gpsimd", sPE, slab_pe[b])
        return p.step("gpsimd", lambda e, bb=b, ff=f0, ww=fw: e.dma_start(
            out=wslab[bb][:, 0:KH * ww].rearrange("p (kc f) -> p kc f", f=ww),
            in_=WinT.rearrange("(kc p) f -> p kc f", p=128)[:, :, ff:ff + ww]), sg, 16)

    slab_ld[0] = load_slab(0, 0)
    slot_i = 0
    cf_global = 0
    xb = 0
    ca = 0
    zs = 0
    xs_i = 0

    for si, (f0, fw, kind) in enumerate(slabs):
        b = si % 2
        if si + 1 < len(slabs):
            slab_ld[(si + 1) % 2] = load_slab(si + 1, (si + 1) % 2)
        nft = fw // 128 if kind != 'dt' else 1
        fwidth = 64 if kind == 'dt' else 128
        for ft in range(nft):
            sessions = [(HALO + tg * 512, 512) for tg in range(4)]
            if kind == 'x':
                sessions = [(0, HALO)] + sessions
            for (hoff, N) in sessions:
                sl = slot_i % 4
                slot_i += 1
                p.wait("tensor", sg, slab_ld[b])
                p.waitt("tensor", slot_cons[sl])
                vmm = None
                for kc in range(KH):
                    fn = (lambda e, bb=b, k=kc, ff=ft, ww=fw, ss=sl, ho=hoff, nn=N,
                          fwd=fwidth, st=(kc == 0), sp=(kc == KH - 1):
                          e.matmul(S(ss, nn, p=fwd),
                                   wslab[bb][:, k * ww + ff * 128: k * ww + ff * 128 + fwd],
                                   HT[:, k * TH + ho: k * TH + ho + nn],
                                   start=st, stop=sp))
                    if kc == KH - 1:
                        vmm = p.step("tensor", fn, sPE)
                    else:
                        p.step("tensor", fn)
                slab_pe[b] = vmm
                if kind == 'z':
                    tg = (hoff - HALO) // 512
                    ftg = f0 // 128 + ft
                    zb = zs % 2
                    zs += 1
                    p.wait("vector", sPE, vmm)
                    p.wait("vector", sq, zfree[zb])
                    vev = p.step("vector", lambda e, ss=sl, zz=zb, nn=N: e.tensor_copy(
                        zstg[zz][:, 0:nn], S(ss, nn)), sDV)
                    slot_cons[sl] = (sDV, vev)
                    p.wait("sync", sDV, vev)
                    zfree[zb] = p.step("sync", lambda e, zz=zb, fg=ftg, t=tg: e.dma_start(
                        out=dZ[fg * 128:(fg + 1) * 128, t * 512:(t + 1) * 512],
                        in_=zstg[zz][:, :]), sq, 16)
                elif kind == 'x':
                    p.wait("vector", sPE, vmm)
                    p.wait("vector", sDV, xbc_cons[xb])
                    vev = p.step("vector", lambda e, ss=sl, bb=xb, ho=hoff, nn=N:
                                 e.tensor_copy(xbcf[bb][:, ho:ho + nn], S(ss, nn)), sDV)
                    slot_cons[sl] = (sDV, vev)
                else:
                    tg = (hoff - HALO) // 512
                    p.wait("vector", sPE, vmm)
                    vev = p.step("vector", lambda e, ss=sl, t=tg, nn=N: e.tensor_copy(
                        dtA[:, t * 512:t * 512 + nn], S(ss, nn, p=64)), sDV)
                    slot_cons[sl] = (sDV, vev)
            if kind == 'x':
                cf = cf_global
                cf_global += 1
                cb = ca % 2
                ca += 1
                p.wait("vector", sq, cw_ld)
                p.wait("vector", sAC, cacc_cons[cb])
                p.step("vector", lambda e, bb=xb, cc=cb, c=cf: e.tensor_scalar(
                    convacc[cc][:, :], xbcf[bb][:, 1:1 + TPC],
                    convwS[:, c * 4:c * 4 + 1], None, ALU.mult))
                vcv = None
                for k in range(1, 4):
                    fn = (lambda e, bb=xb, cc=cb, c=cf, kk=k: e.scalar_tensor_tensor(
                        convacc[cc][:, :], xbcf[bb][:, 1 + kk:1 + kk + TPC],
                        convwS[:, c * 4 + kk:c * 4 + kk + 1], convacc[cc][:, :],
                        ALU.mult, ALU.add))
                    if k == 3:
                        vcv = p.step("vector", fn, sDV)
                    else:
                        p.step("vector", fn)
                xbc_cons[xb] = vcv
                xb = 1 - xb
                p.wait("scalar", sDV, vcv)
                if cf < 32:
                    sxb = xs_i % 2
                    xs_i += 1
                    p.wait("scalar", sa, xsfree[sxb])
                    vsi = p.step("scalar", lambda e, cc=cb, c=cf, bb=sxb: e.activation(
                        xsstg[bb][:, :], convacc[cc][:, :], AF.Silu,
                        bias=convbS[:, c:c + 1]), sAC)
                    cacc_cons[cb] = vsi
                    xsfree[sxb] = p.step("scalar", lambda e, bb=sxb, c=cf: e.dma_start(
                        out=dXS[c * 128:(c + 1) * 128, :], in_=xsstg[bb][:, :]), sa, 16)
                else:
                    dst = Bres if cf == 32 else Cres
                    vsi = p.step("scalar", lambda e, cc=cb, c=cf, dd=dst: e.activation(
                        dd[:, :], convacc[cc][:, :], AF.Silu,
                        bias=convbS[:, c:c + 1]), sAC)
                    cacc_cons[cb] = vsi

    p.wait("scalar", sDV, sDV.n)
    p.wait("scalar", sAC, sAC.n)
    p.wait("vector", sAC, sAC.n)
    p.step("scalar", lambda e: e.activation(t1b[:, :], dtA[:, :], AF.Abs, bias=dtbS[:, :]))
    p.step("scalar", lambda e: e.activation(t1b[:, :], t1b[:, :], AF.Exp, scale=-1.0))
    p.step("scalar", lambda e: e.activation(t1b[:, :], t1b[:, :], AF.Ln, bias=onecolF[0:NH, :]))
    vac = p.step("scalar", lambda e: e.activation(t2b[:, :], dtA[:, :], AF.Relu,
                                                  bias=dtbS[:, :]), sAC)
    p.wait("vector", sAC, vac)
    p.step("vector", lambda e: e.tensor_add(dtFv[:, :], t2b[:, :], t1b[:, :]))
    p.step("vector", lambda e: e.tensor_scalar(t1b[:, :], dtFv[:, :], acol[:, :],
                                               None, ALU.mult))
    for c in range(NCH):
        p.step("vector", lambda e, cc=c: e.tensor_tensor_scan(
            dAcF[:, cc * CS:(cc + 1) * CS], t1b[:, cc * CS:(cc + 1) * CS],
            t1b[:, cc * CS:(cc + 1) * CS], 0.0, ALU.add, ALU.bypass))
    vdv = None
    for c in range(NCH):
        fn = (lambda e, cc=c: e.tensor_scalar(
            dAcR[:, cc * CS:(cc + 1) * CS], dAcF[:, cc * CS:(cc + 1) * CS],
            dAcF[:, cc * CS:cc * CS + 1], None, ALU.subtract))
        if c == NCH - 1:
            vdv = p.step("vector", fn, sDV)
        else:
            p.step("vector", fn)
    p.wait("scalar", sDV, vdv)
    vac = p.step("scalar", lambda e: e.activation(t2b[:, :], dtFv[:, :], AF.Ln), sAC)
    p.step("scalar", lambda e: e.activation(expdAcB[:, :], dAcF[:, :], AF.Exp))
    p.wait("vector", sAC, vac)
    p.step("vector", lambda e: e.tensor_sub(rows2[:, :], t2b[:, :], dAcR[:, :]))
    vdv = None
    for c in range(NCH):
        fn = (lambda e, cc=c: e.tensor_scalar(
            t1b[:, cc * CS:(cc + 1) * CS], dAcF[:, cc * CS:(cc + 1) * CS],
            dAcF[:, cc * CS + CS - 1:cc * CS + CS], -1.0, ALU.subtract, ALU.mult))
        if c == NCH - 1:
            vdv = p.step("vector", fn, sDV)
        else:
            p.step("vector", fn)
    p.wait("scalar", sDV, vdv)
    vac = p.step("scalar", lambda e: e.activation(t1b[:, :], t1b[:, :], AF.Exp), sAC)
    p.wait("vector", sAC, vac)
    p.step("vector", lambda e: e.tensor_mul(cdsB[:, :], t1b[:, :], dtFv[:, :]), sDV)
    p.run("ph2")
    g2.close()
    gH.close()

    g3 = Guards(nc)
    xsc3 = [g3.sb(f"xsc3{i}", (128, 32 * CS), BF16) for i in range(2)]
    xsT3 = g3.sb("xsT3", (128, 2 * DS), BF16)
    xswT = g3.sb("xswT", (128, 2 * DS), BF16)
    wcb = g3.sb("wcb", (128, 2048), BF16)
    BTc = g3.sb("BTc", (128, 256), BF16)
    ststg3 = g3.sb("ststg3", (128, DS), F32)
    rhsS3 = g3.sb("rhsS3", (NH, DS), BF16)
    cE64B3 = g3.sb("cE64B3", (NH, DS), BF16)
    ones64B3 = g3.sb("ones64B3", (NH, 128), BF16)
    dpc3 = g3.sb("dpc3", (NH, 1), F32)
    stgc = g3.sb("stgc", (128, 1), F32)
    stgm = g3.sb("stgm", (128, 1), F32)
    maskQS = g3.sb("maskQS", (128, 4), F32)

    p = Prog(nc, lastwait, selfsems, lastdone)
    barrier(p)
    p.step("gpsimd", lambda e: e.dma_start(out=cE64B3[:, :], in_=cE64), sg, 16)
    p.step("gpsimd", lambda e: e.dma_start(out=ones64B3[:, :], in_=cOnes64F), sg, 16)
    mq_ld = p.step("gpsimd", lambda e: e.dma_start(out=maskQS[:, :], in_=cMaskQD),
                   sg, 16)
    e64_ld = sg.n
    p.step("vector", lambda e: e.memset(carry[:, :], 0.0), sDV)
    xs_ld = [0, 0]
    xs_cons = [0, 0]

    def load_xs3(c, b):
        p.wait("sync", sPE, xs_cons[b])
        return p.step("sync", lambda e, cc=c, bb=b: e.dma_start(
            out=xsc3[bb][:, :].rearrange("p (t l) -> p t l", l=CS),
            in_=dXS.rearrange("(t p) l -> p t l", p=128)[:, :, cc * CS:(cc + 1) * CS]),
            sq, 16)

    xs_ld[0] = load_xs3(0, 0)
    st_dma3 = 0
    for c in range(NCH):
        b = c % 2
        if c + 1 < NCH:
            xs_ld[(c + 1) % 2] = load_xs3(c + 1, (c + 1) % 2)
        p.wait("tensor", sq, xs_ld[b])
        for lh in range(2):
            for grp in range(4):
                bank = (lh * 4 + grp) % 2
                p.waitt("tensor", psb_slot[bank])
                vtp = None
                for j in range(8):
                    i = grp * 8 + j
                    fn = (lambda e, bb=b, ii=i, ll=lh, bk=bank, jj=j: e.transpose(
                        SBk(bk)[:, jj * 128:jj * 128 + 128],
                        xsc3[bb][:, ii * CS + ll * 128: ii * CS + ll * 128 + 128],
                        idb[:, :]))
                    if j == 7:
                        vtp = p.step("tensor", fn, sPE)
                    else:
                        p.step("tensor", fn)
                p.wait("vector", sPE, vtp)
                vev = p.step("vector", lambda e, bk=bank, ll=lh, g_=grp: e.tensor_copy(
                    xsT3[:, ll * DS + g_ * 1024: ll * DS + (g_ + 1) * 1024], SBk(bk)),
                    sDV)
                psb_slot[bank] = (sDV, vev)
        bank = 0
        p.waitt("tensor", psb_slot[bank])
        p.step("tensor", lambda e, cc=c: e.transpose(
            SBk(0)[:, 0:128], Bres[:, cc * CS: cc * CS + 128], idb[:, :]))
        vtp = p.step("tensor", lambda e, cc=c: e.transpose(
            SBk(0)[:, 128:256], Bres[:, cc * CS + 128: cc * CS + 256], idb[:, :]), sPE)
        p.wait("vector", sPE, vtp)
        bt_done = p.step("vector", lambda e: e.tensor_copy(BTc[:, :], SBk(0)[:, 0:256]),
                         sDV)
        psb_slot[bank] = (sDV, bt_done)
        vw_dv = 0
        for lh in range(2):
            for half in range(2):
                p.wait("tensor", sg, e64_ld)
                p.wait("tensor", sAC, sAC.n)
                p.wait("tensor", sDV, sDV.n)
                vmm = None
                for j in range(4):
                    fn = (lambda e, cc=c, ll=lh, hh=half, jj=j: e.matmul(
                        Sv(jj * 512, 512),
                        cdsB[:, cc * CS + ll * 128: cc * CS + ll * 128 + 128],
                        cE64B3[:, hh * 2048 + jj * 512: hh * 2048 + (jj + 1) * 512],
                        start=True, stop=True))
                    if j == 3:
                        vmm = p.step("tensor", fn, sPE)
                    else:
                        p.step("tensor", fn)
                p.wait("scalar", sPE, vmm)
                vev = p.step("scalar", lambda e: e.copy(wcb[:, :], Sv(0, 2048)), sAC)
                p.wait("vector", sAC, vev)
                vw_dv = p.step("vector", lambda e, ll=lh, hh=half: e.tensor_mul(
                    xswT[:, ll * DS + hh * 2048: ll * DS + (hh + 1) * 2048],
                    xsT3[:, ll * DS + hh * 2048: ll * DS + (hh + 1) * 2048],
                    wcb[:, :]), sDV)
        vst_dv = 0
        for half in range(2):
            p.wait("tensor", sDV, sDV.n)
            vmm = None
            for lh in range(2):
                for j in range(4):
                    fn = (lambda e, ll=lh, hh=half, jj=j: e.matmul(
                        Sv(jj * 512, 512), BTc[:, ll * 128:(ll + 1) * 128],
                        xswT[:, ll * DS + hh * 2048 + jj * 512:
                             ll * DS + hh * 2048 + (jj + 1) * 512],
                        start=(ll == 0), stop=(ll == 1)))
                    if lh == 1 and j == 3:
                        vmm = p.step("tensor", fn, sPE)
                    else:
                        p.step("tensor", fn)
            p.wait("vector", sPE, vmm)
            p.wait("vector", sq, st_dma3)
            vst_dv = p.step("vector", lambda e, hh=half: e.tensor_copy(
                ststg3[:, hh * 2048:(hh + 1) * 2048], Sv(0, 2048)), sDV)
        p.wait("sync", sDV, vst_dv)
        st_dma3 = p.step("sync", lambda e, cc=c: e.dma_start(
            out=dST[cc * 128:(cc + 1) * 128, :], in_=ststg3[:, :]), sq, 16)
        if c == 0:
            p.step("vector", lambda e: e.tensor_copy(logdp[:, :], dAcF[:, CS - 1:CS]))
        else:
            p.step("vector", lambda e, cc=c: e.tensor_add(
                logdp[:, :], logdp[:, :], dAcF[:, cc * CS + CS - 1:cc * CS + CS]))
        vac = p.step("scalar", lambda e, cc=c: e.activation(
            dpc3[:, :], dAcF[:, cc * CS + CS - 1:cc * CS + CS], AF.Exp), sAC)
        p.wait("vector", sAC, vac)
        vrh = p.step("vector", lambda e: e.tensor_scalar(
            rhsS3[:, :], cE64B3[:, :], dpc3[:, :], None, ALU.mult), sDV)
        for half in range(2):
            p.wait("tensor", sDV, sDV.n)
            vmm = None
            for j in range(4):
                fn = (lambda e, hh=half, jj=j: e.matmul(
                    Sv(jj * 512, 512), ones64B3[:, :],
                    rhsS3[:, hh * 2048 + jj * 512: hh * 2048 + (jj + 1) * 512],
                    start=True, stop=True))
                if j == 3:
                    vmm = p.step("tensor", fn, sPE)
                else:
                    p.step("tensor", fn)
            p.wait("vector", sPE, vmm)
            p.step("vector", lambda e, hh=half: e.tensor_mul(
                carry[:, hh * 2048:(hh + 1) * 2048],
                carry[:, hh * 2048:(hh + 1) * 2048], Sv(0, 2048)))
            p.step("vector", lambda e, hh=half: e.tensor_add(
                carry[:, hh * 2048:(hh + 1) * 2048],
                carry[:, hh * 2048:(hh + 1) * 2048],
                ststg3[:, hh * 2048:(hh + 1) * 2048]), sDV)
        xs_cons[b] = sPE.n

    p.step("vector", lambda e: e.memset(stgc[:, :], 0.0))
    p.wait("vector", sg, mq_ld)
    vdv = p.step("vector", lambda e: e.tensor_copy(stgc[0:NH, :], logdp[:, :]), sDV)
    last_dma = st_dma3
    for j in range(4):
        p.wait("vector", sq, last_dma)
        p.step("vector", lambda e, jj=j: e.tensor_scalar(
            ststg3[:, :], carry[:, :], maskQS[:, jj:jj + 1], None, ALU.mult))
        v2m = p.step("vector", lambda e, jj=j: e.tensor_scalar(
            stgm[:, :], stgc[:, :], maskQS[:, jj:jj + 1], None, ALU.mult), sDV)
        p.wait("sync", sDV, v2m)
        p.step("sync", lambda e, jj=j: e.dma_start(
            out=dAGi[jj * 128:(jj + 1) * 128, 0:DS], in_=ststg3[:, :]), sq, 16)
        last_dma = p.step("sync", lambda e, jj=j: e.dma_start(
            out=dAGi[jj * 128:(jj + 1) * 128, DS:DS + 1], in_=stgm[:, :],
            allow_slow_non_contiguous=True), sq, 16)
    p.wait("gpsimd", sq, last_dma)
    p.step("gpsimd", lambda e: e.collective_compute(
        "AllReduce", ALU.add,
        replica_groups=[[0, 1, 2, 3], [4, 5, 6, 7]],
        ins=[dAGi.opt()], outs=[dAGo.opt()]), sCC, 1)
    p.step("gpsimd", lambda e: e.dma_start(out=dPR[0:128, :], in_=Bres[:, :]), sg, 16)
    p.step("gpsimd", lambda e: e.dma_start(out=dPR[128:256, :], in_=Cres[:, :]), sg, 16)
    p.step("gpsimd", lambda e: e.dma_start(out=dPR[384:448, :], in_=expdAcB[:, :]),
           sg, 16)
    p.step("sync", lambda e: e.dma_start(out=dPR[256:320, :], in_=dAcR[:, :]), sq, 16)
    p.step("sync", lambda e: e.dma_start(out=dPR[320:384, :], in_=rows2[:, :]), sq, 16)
    p.step("sync", lambda e: e.dma_start(out=dPR[448:512, :], in_=dAcF[:, :]), sq, 16)
    p.run("ph3")
    g3.close()
    gA.close()

    g4 = Guards(nc)
    XTres = g4.sb("XTres", (128, KH * TPC), BF16)
    actbuf = g4.sb("actbuf", (128, 11 * TPC), BF16)
    pairslab = [g4.sb(f"pairslab{i}", (128, KH * 512), BF16) for i in range(2)]
    w2slab = [g4.sb(f"w2slab{i}", (128, 11 * 512), BF16) for i in range(2)]
    cbuf = g4.sb("cbuf", (128, TPC), F32)
    combT = g4.sb("combT", (16, TPC), F32)
    combAll = g4.sb("combAll", (128, 16 * 16), F32)
    gslab = g4.sb("gslab", (128, KH * 16), BF16)
    sel16S = g4.sb("sel16S", (16, 16 * 128), F32)
    gsil = [g4.sb(f"gsil{i}", (128, 512), F32) for i in range(2)]
    sgcb = [g4.sb(f"sgcb{i}", (128, 512), F32) for i in range(2)]
    ostg = [g4.sb(f"ostg{i}", (128, 512), F32) for i in range(4)]
    esm4 = g4.sb("esm4", (128, 16), F32)
    scb4 = g4.sb("scb4", (128, 16), F32)
    mkb4 = g4.sb("mkb4", (128, 16), F32)
    m8b = g4.sb("m8b", (128, 8), F32)
    cc1 = g4.sb("cc1", (128, 1), F32)
    cc2 = g4.sb("cc2", (128, 1), F32)
    cc3 = g4.sb("cc3", (128, 1), F32)

    p = Prog(nc, lastwait, selfsems, lastdone)
    barrier(p)
    p.step("sync", lambda e: e.dma_start(out=XTres[:, :], in_=dXT), sq, 16)
    p.step("gpsimd", lambda e: e.dma_start(
        out=gslab[:, :].rearrange("p (kc f) -> p kc f", f=16),
        in_=gateTd.rearrange("(kc p) f -> p kc f", p=128)), sg, 16)
    p.step("sync", lambda e: e.dma_start(out=sel16S[:, :], in_=cSel16F), sq, 16)
    xt_ld = sq.n
    g_ld = sg.n

    vac_prev = 0
    for tt in range(16):
        sl = tt % 2
        p.wait("tensor", sq, xt_ld)
        p.wait("tensor", sg, g_ld)
        p.waitt("tensor", slot_cons[sl])
        vmm = None
        for kc in range(KH):
            fn = (lambda e, k=kc, t=tt, ss=sl: e.matmul(
                S(ss, 16), XTres[:, k * TPC + t * 128: k * TPC + (t + 1) * 128],
                gslab[:, k * 16:(k + 1) * 16], start=(k == 0), stop=(k == KH - 1)))
            if kc == KH - 1:
                vmm = p.step("tensor", fn, sPE)
            else:
                p.step("tensor", fn)
        p.wait("vector", sPE, vmm)
        p.wait("vector", sAC, vac_prev)
        vdv = p.step("vector", lambda e, ss=sl: e.reduce_max(
            cc1[:, :], S(ss, 16), axis=AXX, negate=True), sDV)
        p.wait("scalar", sDV, vdv)
        vac = p.step("scalar", lambda e, ss=sl: e.activation(
            esm4[:, :], S(ss, 16), AF.Exp, bias=cc1[:, :]), sAC)
        vac_prev = vac
        slot_cons[sl] = (sAC, vac)
        p.wait("vector", sAC, vac)
        p.step("vector", lambda e: e.reduce_sum(cc2[:, :], esm4[:, :], axis=AXX))
        p.step("vector", lambda e: e.reciprocal(cc3[:, :], cc2[:, :]))
        p.step("vector", lambda e: e.tensor_scalar(scb4[:, :], esm4[:, :], cc3[:, :],
                                                   None, ALU.mult))
        p.step("vector", lambda e: e.max(m8b[:, :], scb4[:, :]))
        p.step("vector", lambda e: e.reduce_sum(cc2[:, :], m8b[:, 0:6], axis=AXX))
        p.step("vector", lambda e: e.tensor_scalar_add(cc2[:, :], cc2[:, :], 1e-20))
        p.step("vector", lambda e: e.reciprocal(cc3[:, :], cc2[:, :]))
        p.step("vector", lambda e: e.tensor_scalar(mkb4[:, :], scb4[:, :],
                                                   m8b[:, 5:6], None, ALU.is_ge))
        p.step("vector", lambda e, t=tt: e.scalar_tensor_tensor(
            combAll[:, t * 16:(t + 1) * 16], scb4[:, :], cc3[:, :], mkb4[:, :],
            ALU.mult, ALU.mult), sDV)
    comb_dv = sDV.n
    for tt in range(16):
        p.waitt("tensor", slot_cons[2])
        p.wait("tensor", sDV, comb_dv)
        vmm = p.step("tensor", lambda e, t=tt: e.transpose(
            S(2, 128, p=16), combAll[:, t * 16:(t + 1) * 16], idf[:, :]), sPE)
        p.wait("vector", sPE, vmm)
        vdv = p.step("vector", lambda e, t=tt: e.tensor_copy(
            combT[:, t * 128:(t + 1) * 128], S(2, 128, p=16)), sDV)
        slot_cons[2] = (sDV, vdv)
    combT_dv = sDV.n

    slabjobs = []
    for ei in range(NE + 1):
        shared = (ei == NE)
        nth = 2 if shared else 1
        pairs = ([(pi * 256, 256) for pi in range(11)] if shared else
                 [(pi * 256, 256) for pi in range(5)] + [(1280, 128)])
        for th in range(nth):
            for (g0, pw) in pairs:
                slabjobs.append((ei, th, g0, pw))

    pair_pe = [0, 0]
    pair_ld = [0, 0]
    w2_pe_g = [0, 0]
    act_war = 0

    def load_pair(idx):
        ei, th, g0, pw = slabjobs[idx]
        b = idx % 2
        shared = (ei == NE)
        ub = FIS if shared else FI
        wsrc = sw13Td if shared else w13Td[ei * H:(ei + 1) * H, :]
        p.wait("gpsimd", sPE, pair_pe[b])
        p.step("gpsimd", lambda e, bb=b, ws=wsrc, gg=g0, ww=pw: e.dma_start(
            out=wslab4_view(pairslab[bb], ww, 0),
            in_=ws.rearrange("(kc p) f -> p kc f", p=128)[:, :, gg:gg + ww]), sg, 16)
        v = p.step("gpsimd", lambda e, bb=b, ws=wsrc, gg=g0, ww=pw, uu=ub: e.dma_start(
            out=wslab4_view(pairslab[bb], ww, 256),
            in_=ws.rearrange("(kc p) f -> p kc f", p=128)[:, :, uu + gg:uu + gg + ww]),
            sg, 16)
        pair_ld[b] = v

    def wslab4_view(t, pw, off):
        return t[:, :].rearrange("p (kc f) -> p kc f", f=512)[:, :, off:off + pw]

    load_pair(0)
    jobi = 0
    gu_rot = 0
    o_rot = 0
    os_i = 0
    gs_i = 0
    sgc_i = 0
    gsil_cons = [0, 0]
    gsil_sem = [sPO, sPO]
    sgc_cons = [0, 0]
    ost_free = [0] * 4
    accum_prev = 0
    cb_dv = 0
    cbuf_pool_cons = 0

    for ei in range(NE + 1):
        shared = (ei == NE)
        nth = 2 if shared else 1
        nit = 22 if shared else 11
        tgrp = 1024 if shared else TPC
        pairs = ([(pi * 256, 256) for pi in range(11)] if shared else
                 [(pi * 256, 256) for pi in range(5)] + [(1280, 128)])
        if not shared:
            for i4 in range(4):
                p.waitt("tensor", slot_cons[i4])
            p.wait("tensor", sq, sq.n)
            p.wait("tensor", sDV, combT_dv)
            vmm = None
            for j in range(4):
                fn = (lambda e, ee=ei, jj=j: e.matmul(
                    Sv(jj * 512, 512), sel16S[:, ee * 128:(ee + 1) * 128],
                    combT[:, jj * 512:(jj + 1) * 512], start=True, stop=True))
                if j == 3:
                    vmm = p.step("tensor", fn, sPE)
                else:
                    p.step("tensor", fn)
            p.wait("vector", sPE, vmm)
            p.wait("vector", sPO, cbuf_pool_cons)
            cb_dv = p.step("vector", lambda e: e.tensor_copy(cbuf[:, :], Sv(0, 2048)),
                           sDV)
            for i4 in range(4):
                slot_cons[i4] = (sDV, cb_dv)
        for th in range(nth):
            for (g0, pw) in pairs:
                b = jobi % 2
                jobi += 1
                if jobi < len(slabjobs):
                    load_pair(jobi)
                for gf in range(pw // 128):
                    it = (g0 // 128) + gf
                    for tg in range(4 // nth):
                        t0 = th * 1024 + tg * 512
                        sG = gu_rot % 2
                        sU = 2 + gu_rot % 2
                        gu_rot += 1
                        p.wait("tensor", sg, pair_ld[b])
                        p.waitt("tensor", slot_cons[sG])
                        vG = None
                        for kc in range(KH):
                            fn = (lambda e, bb=b, k=kc, g_=gf, tt0=t0, ss=sG:
                                  e.matmul(S(ss),
                                           pairslab[bb][:, k * 512 + g_ * 128:
                                                        k * 512 + g_ * 128 + 128],
                                           XTres[:, k * TPC + tt0: k * TPC + tt0 + 512],
                                           start=(k == 0), stop=(k == KH - 1)))
                            if kc == KH - 1:
                                vG = p.step("tensor", fn, sPE)
                            else:
                                p.step("tensor", fn)
                        p.waitt("tensor", slot_cons[sU])
                        vU = None
                        for kc in range(KH):
                            fn = (lambda e, bb=b, k=kc, g_=gf, tt0=t0, ss=sU:
                                  e.matmul(S(ss),
                                           pairslab[bb][:, k * 512 + 256 + g_ * 128:
                                                        k * 512 + 256 + g_ * 128 + 128],
                                           XTres[:, k * TPC + tt0: k * TPC + tt0 + 512],
                                           start=(k == 0), stop=(k == KH - 1)))
                            if kc == KH - 1:
                                vU = p.step("tensor", fn, sPE)
                            else:
                                p.step("tensor", fn)
                        pair_pe[b] = vU
                        gb = gs_i % 2
                        gs_i += 1
                        p.wait("scalar", sPE, vG)
                        p.waitt("scalar", (gsil_sem[gb], gsil_cons[gb]))
                        vGS = p.step("scalar", lambda e, ss=sG, bb=gb: e.activation(
                            gsil[bb][:, :], S(ss), AF.Silu), sAC)
                        slot_cons[sG] = (sAC, vGS)
                        if not shared:
                            pb = sgc_i % 2
                            sgc_i += 1
                            p.wait("gpsimd", sAC, vGS)
                            p.wait("gpsimd", sDV, cb_dv)
                            p.wait("gpsimd", sDV, sgc_cons[pb])
                            vSG = p.step("gpsimd", lambda e, bb=gb, pp=pb, tt0=t0:
                                         e.tensor_mul(sgcb[pp][:, :], gsil[bb][:, :],
                                                      cbuf[:, tt0:tt0 + 512]), sPO)
                            cbuf_pool_cons = vSG
                            gsil_cons[gb] = vSG
                            gsil_sem[gb] = sPO
                            p.wait("vector", sPE, vU)
                            p.wait("vector", sPE, act_war)
                            p.wait("vector", sPO, vSG)
                            acol0 = it * tgrp + t0
                            vAT = p.step("vector", lambda e, ss=sU, pp=pb, a0=acol0:
                                         e.scalar_tensor_tensor(
                                             actbuf[:, a0:a0 + 512],
                                             S(ss), 1.0, sgcb[pp][:, :],
                                             ALU.mult, ALU.mult), sDV)
                            sgc_cons[pb] = vAT
                            slot_cons[sU] = (sDV, vAT)
                        else:
                            p.wait("vector", sPE, vU)
                            p.wait("vector", sPE, act_war)
                            p.wait("vector", sAC, vGS)
                            acol0 = it * tgrp + (t0 - th * 1024)
                            vAT = p.step("vector", lambda e, ss=sU, bb=gb, a0=acol0:
                                         e.scalar_tensor_tensor(
                                             actbuf[:, a0:a0 + 512],
                                             S(ss), 1.0, gsil[bb][:, :],
                                             ALU.mult, ALU.mult), sDV)
                            gsil_cons[gb] = vAT
                            gsil_sem[gb] = sDV
                            slot_cons[sU] = (sDV, vAT)
            act_done = sDV.n
            for fc in range(4):
                if shared:
                    p.wait("gpsimd", sPE, max(w2_pe_g))
                    p.step("gpsimd", lambda e, ff=fc: e.dma_start(
                        out=w2slab[0][:, :].rearrange("p (kc f) -> p kc f", f=512),
                        in_=sw2Td[0:11 * 128, ff * 512:(ff + 1) * 512].rearrange(
                            "(kc p) f -> p kc f", p=128)), sg, 16)
                    vw2 = p.step("gpsimd", lambda e, ff=fc: e.dma_start(
                        out=w2slab[1][:, :].rearrange("p (kc f) -> p kc f", f=512),
                        in_=sw2Td[11 * 128:22 * 128, ff * 512:(ff + 1) * 512].rearrange(
                            "(kc p) f -> p kc f", p=128)), sg, 16)
                    w2_ld_v = vw2
                else:
                    wb = fc % 2
                    p.wait("gpsimd", sPE, w2_pe_g[wb])
                    w2_ld_v = p.step("gpsimd", lambda e, ee=ei, ff=fc, bb=wb: e.dma_start(
                        out=w2slab[bb][:, :].rearrange("p (kc f) -> p kc f", f=512),
                        in_=w2Td[ee * FI:(ee + 1) * FI, ff * 512:(ff + 1) * 512].rearrange(
                            "(kc p) f -> p kc f", p=128)), sg, 16)
                for ttl in range(16 // nth):
                    tt = th * 8 + ttl if shared else ttl
                    sO = 4 + o_rot % 2
                    o_rot += 1
                    p.wait("tensor", sg, w2_ld_v)
                    p.waitt("tensor", slot_cons[sO])
                    p.wait("tensor", sDV, act_done)
                    vO = None
                    for kc in range(nit):
                        wsl = w2slab[kc // 11] if shared else w2slab[fc % 2]
                        kcc = kc % 11 if shared else kc
                        fn = (lambda e, k=kc, kk=kcc, ws=wsl, ss=sO, tl=ttl, tg_=tgrp,
                              ni=nit:
                              e.matmul(S(ss),
                                       actbuf[:, k * tg_ + tl * 128: k * tg_ + tl * 128 + 128],
                                       ws[:, kk * 512:(kk + 1) * 512],
                                       start=(k == 0), stop=(k == ni - 1)))
                        if kc == nit - 1:
                            vO = p.step("tensor", fn, sPE)
                        else:
                            p.step("tensor", fn)
                    if shared:
                        w2_pe_g = [vO, vO]
                    else:
                        w2_pe_g[fc % 2] = vO
                    ob = os_i % 4
                    eng = "vector" if os_i % 2 == 0 else "scalar"
                    esem = sDV if os_i % 2 == 0 else sAC
                    os_i += 1
                    p.wait(eng, sPE, vO)
                    p.wait(eng, sg, ost_free[ob])
                    if eng == "vector":
                        vE = p.step(eng, lambda e, ss=sO, bb=ob: e.tensor_copy(
                            ostg[bb][:, :], S(ss)), esem)
                    else:
                        vE = p.step(eng, lambda e, ss=sO, bb=ob: e.copy(
                            ostg[bb][:, :], S(ss)), esem)
                    slot_cons[sO] = (esem, vE)
                    p.wait("gpsimd", esem, vE)
                    p.wait("gpsimd", sg, accum_prev)
                    aop = ALU.bypass if ei == 0 else ALU.add
                    ost_free[ob] = p.step("gpsimd", lambda e, bb=ob, t=tt, ff=fc,
                                          op=aop: e.dma_start(
                        out=dG[t * 128:(t + 1) * 128, ff * 512:(ff + 1) * 512],
                        in_=ostg[bb][:, :], accum_op=op), sg, 16)
            act_war = sPE.n
        accum_prev = sg.n
    p.run("ph4")
    g4.close()

    g5 = Guards(nc)
    carry2 = g5.sb("carry2", (128, DS), F32)
    carryB = g5.sb("carryB", (128, DS), BF16)
    xsc5 = g5.sb("xsc5", (128, 32 * CS), BF16)
    xsT5 = g5.sb("xsT5", (128, 2 * DS), BF16)
    szcb = g5.sb("szcb", (128, 32 * CS), BF16)
    ybuf = g5.sb("ybuf", (128, 32 * CS), BF16)
    ygb = g5.sb("ygb", (128, 32 * CS), BF16)
    ststg5 = g5.sb("ststg5", (128, DS), F32)
    BresR = g5.sb("BresR", (128, TPC), BF16)
    CresR = g5.sb("CresR", (128, TPC), BF16)
    dAcRr = g5.sb("dAcRr", (NH, TPC), F32)
    rows2r = g5.sb("rows2r", (NH, TPC), F32)
    dAcFr = g5.sb("dAcFr", (NH, TPC), F32)
    expdAcr = g5.sb("expdAcr", (NH, TPC), BF16)
    Ddiag = g5.sb("Ddiag", (128, 32 * 128), BF16)
    selPairS = g5.sb("selPairS", (NH, 32 * 128), BF16)
    cE64B5 = g5.sb("cE64B5", (NH, DS), BF16)
    ones64B5 = g5.sb("ones64B5", (NH, 128), BF16)
    ones64F5 = g5.sb("ones64F5", (NH, 128), F32)
    rhsS5 = g5.sb("rhsS5", (NH, DS), BF16)
    rhm = [g5.sb(f"rhm{i}", (NH, CS), F32) for i in range(2)]
    Ebuf = [g5.sb(f"Ebuf{i}", (128, 384), BF16) for i in range(2)]
    Mstg = [g5.sb(f"Mstg{i}", (128, 384), BF16) for i in range(2)]
    EAstg = [g5.sb(f"EAstg{i}", (128, CS), F32) for i in range(2)]
    t1stg = [g5.sb(f"t1stg{i}", (128, CS), F32) for i in range(2)]
    CBTm = g5.sb("CBTm", (128, 384), BF16)
    Ecl = [g5.sb(f"Ecl{i}", (128, 384), F32) for i in range(2)]
    trilS = g5.sb("trilS", (128, 384), F32)
    r2T = g5.sb("r2T", (128, 128), F32)
    dpc5 = g5.sb("dpc5", (NH, 1), F32)
    rsr = g5.sb("rsr", (1, CS), F32)
    tselS = g5.sb("tselS", (NH, 4), F32)
    tselCS = g5.sb("tselCS", (NH, 4), F32)
    maskBS = g5.sb("maskBS", (128, 4), F32)
    tcol5 = g5.sb("tcol5", (NH, 1), F32)

    p = Prog(nc, lastwait, selfsems, lastdone)
    barrier(p)
    p.step("gpsimd", lambda e: e.dma_start(out=BresR[:, :], in_=dPR[0:128, :]), sg, 16)
    p.step("gpsimd", lambda e: e.dma_start(out=CresR[:, :], in_=dPR[128:256, :]), sg, 16)
    p.step("gpsimd", lambda e: e.dma_start(out=expdAcr[:, :], in_=dPR[384:448, :]),
           sg, 16)
    p.step("gpsimd", lambda e: e.dma_start(out=selPairS[:, :], in_=cSelPair), sg, 16)
    p.step("gpsimd", lambda e: e.dma_start(out=cE64B5[:, :], in_=cE64), sg, 16)
    p.step("gpsimd", lambda e: e.dma_start(out=ones64B5[:, :], in_=cOnes64F), sg, 16)
    p.step("sync", lambda e: e.dma_start(out=dAcRr[:, :], in_=dPR[256:320, :]), sq, 16)
    p.step("sync", lambda e: e.dma_start(out=rows2r[:, :], in_=dPR[320:384, :]), sq, 16)
    p.step("sync", lambda e: e.dma_start(out=dAcFr[:, :], in_=dPR[448:512, :]), sq, 16)
    p.step("sync", lambda e: e.dma_start(out=ones64F5[:, :], in_=cOnes64F), sq, 16)
    p.step("sync", lambda e: e.dma_start(out=trilS[:, :], in_=cTril), sq, 16)
    p.step("sync", lambda e: e.dma_start(out=tselS[:, :], in_=cTselD), sq, 16)
    p.step("sync", lambda e: e.dma_start(out=tselCS[:, :], in_=cTselCD), sq, 16)
    p.step("sync", lambda e: e.dma_start(out=maskBS[:, :], in_=cMaskB), sq, 16)
    ld5_q = sq.n
    ld5_g = sg.n
    p.wait("vector", sg, sg.n)
    for i in range(32):
        p.step("vector", lambda e, ii=i: e.tensor_scalar(
            Ddiag[:, ii * 128:(ii + 1) * 128], idb[:, :], dcol32[:, ii:ii + 1],
            None, ALU.mult))
    p.step("vector", lambda e: e.memset(carry2[:, :], 0.0), sDV)

    p.wait("sync", sCC, 1)
    for j in range(4):
        p.wait("sync", sDV, sDV.n)
        vl1 = p.step("sync", lambda e, jj=j: e.dma_start(
            out=ststg5[:, :], in_=dAGo[jj * 128:(jj + 1) * 128, 0:DS]), sq, 16)
        vl2 = p.step("sync", lambda e, jj=j: e.dma_start(
            out=tcol5[:, :], in_=dAGo[jj * 128:jj * 128 + NH, DS:DS + 1],
            allow_slow_non_contiguous=True), sq, 16)
        p.wait("scalar", sq, vl2)
        vac = p.step("scalar", lambda e: e.activation(dpc5[:, :], tcol5[:, :], AF.Exp),
                     sAC)
        p.wait("vector", sAC, vac)
        p.wait("vector", sq, ld5_q)
        p.wait("vector", sg, ld5_g)
        p.step("vector", lambda e, jj=j: e.tensor_scalar(
            dpc5[:, :], dpc5[:, :], tselS[:, jj:jj + 1], tselCS[:, jj:jj + 1],
            ALU.mult, ALU.add))
        vrh = p.step("vector", lambda e: e.tensor_scalar(
            rhsS5[:, :], cE64B5[:, :], dpc5[:, :], None, ALU.mult), sDV)
        for half in range(2):
            p.wait("tensor", sDV, sDV.n)
            p.wait("tensor", sq, vl1)
            vmm = None
            for jj in range(4):
                fn = (lambda e, hh=half, j4=jj: e.matmul(
                    Sv(j4 * 512, 512), ones64B5[:, :],
                    rhsS5[:, hh * 2048 + j4 * 512: hh * 2048 + (j4 + 1) * 512],
                    start=True, stop=True))
                if jj == 3:
                    vmm = p.step("tensor", fn, sPE)
                else:
                    p.step("tensor", fn)
            p.wait("vector", sPE, vmm)
            p.step("vector", lambda e, hh=half: e.tensor_mul(
                carry2[:, hh * 2048:(hh + 1) * 2048],
                carry2[:, hh * 2048:(hh + 1) * 2048], Sv(0, 2048)))
            p.step("vector", lambda e, hh=half, jj=j: e.scalar_tensor_tensor(
                carry2[:, hh * 2048:(hh + 1) * 2048],
                ststg5[:, hh * 2048:(hh + 1) * 2048], maskBS[:, jj:jj + 1],
                carry2[:, hh * 2048:(hh + 1) * 2048], ALU.mult, ALU.add), sDV)

    xs_pe = 0
    zc_ac = 0
    st_dv = sDV.n
    ygd_prev = 0
    ss_prev = 0
    rsr_dma = 0
    rhm_pe = [0, 0]
    Eb_dv = [0, 0]
    ecl_cons = [0, 0]
    Ms_pe = [0, 0]
    EA_dv = [0, 0]
    yd_cons = [None, None]
    yo_cons = None
    r_cons = [None, None]
    s0_cons = None
    cb5_pe = 0

    for c in range(NCH):
        p.wait("sync", sPE, xs_pe)
        vxs = p.step("sync", lambda e, cc=c: e.dma_start(
            out=xsc5[:, :].rearrange("p (t l) -> p t l", l=CS),
            in_=dXS.rearrange("(t p) l -> p t l", p=128)[:, :, cc * CS:(cc + 1) * CS]),
            sq, 16)
        p.wait("sync", sAC, zc_ac)
        vzc = p.step("sync", lambda e, cc=c: e.dma_start(
            out=szcb[:, :].rearrange("p (t l) -> p t l", l=CS),
            in_=dZ.rearrange("(t p) l -> p t l", p=128)[:, :, cc * CS:(cc + 1) * CS]),
            sq, 16)
        p.wait("sync", sDV, st_dv)
        vst = p.step("sync", lambda e, cc=c: e.dma_start(
            out=ststg5[:, :], in_=dST[cc * 128:(cc + 1) * 128, :]), sq, 16)
        p.wait("vector", sPE, cb5_pe)
        vcb = p.step("vector", lambda e: e.tensor_copy(carryB[:, :], carry2[:, :]), sDV)
        p.wait("tensor", sq, vxs)
        for lh in range(2):
            for grp in range(4):
                bank = (lh * 4 + grp) % 2
                p.waitt("tensor", psb_slot[bank])
                vtp = None
                for j in range(8):
                    i = grp * 8 + j
                    fn = (lambda e, ii=i, ll=lh, bk=bank, jj=j: e.transpose(
                        SBk(bk)[:, jj * 128:jj * 128 + 128],
                        xsc5[:, ii * CS + ll * 128: ii * CS + ll * 128 + 128],
                        idb[:, :]))
                    if j == 7:
                        vtp = p.step("tensor", fn, sPE)
                    else:
                        p.step("tensor", fn)
                p.wait("vector", sPE, vtp)
                vev = p.step("vector", lambda e, bk=bank, ll=lh, g_=grp: e.tensor_copy(
                    xsT5[:, ll * DS + g_ * 1024: ll * DS + (g_ + 1) * 1024], SBk(bk)),
                    sDV)
                psb_slot[bank] = (sDV, vev)
        xsT_dv = sDV.n
        p.waitt("tensor", r_cons[1])
        p.wait("tensor", sq, ld5_q)
        p.step("tensor", lambda e, cc=c: e.transpose(
            S(3, 64), rows2r[0:NH, cc * CS: cc * CS + 128], idf[0:NH, 0:NH]))
        vtp = p.step("tensor", lambda e, cc=c: e.transpose(
            S(3, 64)[:, :] if False else psF[0:128, 3 * 512 + 64:3 * 512 + 128],
            rows2r[0:NH, cc * CS + 128: cc * CS + 256], idf[0:NH, 0:NH]), sPE)
        p.wait("vector", sPE, vtp)
        vr2 = p.step("vector", lambda e: e.tensor_copy(r2T[:, :], S(3, 128)), sDV)
        r_cons[1] = (sDV, vr2)
        p.waitt("tensor", s0_cons)
        p.wait("tensor", sg, ld5_g)
        p.step("tensor", lambda e, cc=c: e.matmul(
            S(0, 256), BresR[:, cc * CS: cc * CS + 128],
            CresR[:, cc * CS: cc * CS + 256], start=True, stop=True))
        vmm = p.step("tensor", lambda e, cc=c: e.matmul(
            Sv(256, 128), BresR[:, cc * CS + 128: cc * CS + 256],
            CresR[:, cc * CS + 128: cc * CS + 256], start=True, stop=True), sPE)
        p.wait("vector", sPE, vmm)
        p.wait("vector", sq, ld5_q)
        p.step("vector", lambda e: e.tensor_mul(CBTm[:, 0:256], S(0, 256),
                                                trilS[:, 0:256]))
        vcbt = p.step("vector", lambda e: e.tensor_mul(CBTm[:, 256:384], Sv(256, 128),
                                                       trilS[:, 256:384]), sDV)
        s0_cons = (sDV, vcbt)
        for i in range(32):
            ydslot = 4 + (i % 2)
            ebank = i % 2
            for hi in range(2):
                h = 2 * i + hi
                rslot = 1 if hi == 0 else 3
                p.wait("vector", sPE, rhm_pe[hi])
                vrhm = p.step("vector", lambda e, cc=c, hh=h, b2=hi: e.tensor_scalar(
                    rhm[b2][:, :], dAcRr[:, cc * CS:(cc + 1) * CS],
                    idf[0:NH, hh:hh + 1], None, ALU.mult), sDV)
                p.wait("tensor", sDV, vrhm)
                p.waitt("tensor", r_cons[hi])
                vR = p.step("tensor", lambda e, b2=hi, rs=rslot: e.matmul(
                    S(rs, 256), ones64F5[:, :], rhm[b2][:, :], start=True, stop=True),
                    sPE)
                rhm_pe[hi] = vR
                p.wait("vector", sPE, vR)
                p.wait("vector", sDV, vr2)
                p.wait("vector", sAC, ecl_cons[hi])
                p.step("vector", lambda e, b2=hi, rs=rslot, hh=h: e.tensor_scalar(
                    Ecl[b2][:, 0:256], S(rs, 256), r2T[:, hh:hh + 1], 30.0,
                    ALU.add, ALU.min))
                vcl = p.step("vector", lambda e, b2=hi, rs=rslot, hh=h:
                             e.tensor_scalar(
                    Ecl[b2][:, 256:384], Sv(rs * 512 + 128, 128),
                    r2T[:, 64 + hh:64 + hh + 1], 30.0, ALU.add, ALU.min), sDV)
                p.wait("scalar", sDV, vcl)
                p.wait("scalar", sDV, Eb_dv[hi])
                p.step("scalar", lambda e, b2=hi: e.activation(
                    Ebuf[b2][:, 0:256], Ecl[b2][:, 0:256], AF.Exp))
                vE = p.step("scalar", lambda e, b2=hi: e.activation(
                    Ebuf[b2][:, 256:384], Ecl[b2][:, 256:384], AF.Exp), sAC)
                r_cons[hi] = (sDV, vcl)
                ecl_cons[hi] = vE
                p.wait("vector", sAC, vE)
                p.wait("vector", sPE, Ms_pe[hi])
                p.step("vector", lambda e, b2=hi: e.tensor_mul(
                    Mstg[b2][:, 0:256], Ebuf[b2][:, 0:256], CBTm[:, 0:256]))
                vM = p.step("vector", lambda e, b2=hi: e.tensor_mul(
                    Mstg[b2][:, 256:384], Ebuf[b2][:, 256:384], CBTm[:, 256:384]), sDV)
                Eb_dv[hi] = vM
            p.wait("tensor", sDV, Eb_dv[0])
            p.wait("tensor", sDV, Eb_dv[1])
            p.wait("tensor", sDV, xsT_dv)
            p.waitt("tensor", yd_cons[i % 2])
            vyd = None
            for hi in range(2):
                h = 2 * i + hi
                pbase = 64 * hi
                p.step("tensor", lambda e, b2=hi, hh=h, ys=ydslot, pb=pbase:
                       e.matmul(
                    psF[pb:pb + 64, ys * 512:ys * 512 + 256],
                    xsT5[:, hh * 64: hh * 64 + 64], Mstg[b2][:, 0:256],
                    start=True, stop=False))
                p.step("tensor", lambda e, b2=hi, hh=h, ys=ydslot, pb=pbase:
                       e.matmul(
                    psF[pb:pb + 64, ys * 512 + 128:ys * 512 + 256],
                    xsT5[:, DS + hh * 64: DS + hh * 64 + 64], Mstg[b2][:, 256:384],
                    start=False, stop=False))
                vyd = p.step("tensor", lambda e, ii=i, ys=ydslot, cc=c, pb=pbase:
                             e.matmul(
                    psF[pb:pb + 64, ys * 512:ys * 512 + 256],
                    Ddiag[:, ii * 128 + pb:ii * 128 + pb + 64],
                    xsc5[:, ii * CS:(ii + 1) * CS], start=False, stop=True), sPE)
                Ms_pe[hi] = vyd
            p.waitt("tensor", yo_cons)
            vea = p.step("tensor", lambda e, ii=i, cc=c: e.matmul(
                S(2, 256), selPairS[:, ii * 128:(ii + 1) * 128],
                expdAcr[:, cc * CS:(cc + 1) * CS], start=True, stop=True), sPE)
            p.wait("scalar", sPE, vea)
            p.wait("scalar", sDV, EA_dv[i % 2])
            vEA = p.step("scalar", lambda e, b2=i % 2: e.copy(
                EAstg[b2][:, :], S(2, 256)), sAC)
            p.wait("tensor", sAC, vEA)
            p.wait("tensor", sDV, vcb)
            vyo = p.step("tensor", lambda e, ii=i, cc=c: e.matmul(
                S(2, 256), carryB[:, ii * 128:(ii + 1) * 128],
                CresR[:, cc * CS:(cc + 1) * CS], start=True, stop=True), sPE)
            p.wait("vector", sPE, vyo)
            p.wait("vector", sAC, vEA)
            vt1 = p.step("vector", lambda e, b2=i % 2: e.tensor_mul(
                t1stg[b2][:, :], S(2, 256), EAstg[b2][:, :]), sDV)
            yo_cons = (sDV, vt1)
            EA_dv[i % 2] = vt1
            p.wait("vector", sPE, vyd)
            p.wait("vector", sPE, ss_prev)
            vy = p.step("vector", lambda e, ii=i, ys=ydslot, b2=i % 2:
                        e.scalar_tensor_tensor(
                ybuf[:, ii * CS:(ii + 1) * CS], S(ys, 256), 1.0, t1stg[b2][:, :],
                ALU.mult, ALU.add), sDV)
            yd_cons[i % 2] = (sDV, vy)
        xs_pe = sPE.n
        yall_dv = sDV.n
        p.wait("scalar", sq, vzc)
        p.wait("scalar", sDV, yall_dv)
        vsz = p.step("scalar", lambda e: e.activation(szcb[:, :], szcb[:, :], AF.Silu),
                     sAC)
        zc_ac = vsz
        p.wait("vector", sAC, vsz)
        p.wait("vector", sDV, yall_dv)
        p.wait("vector", sa, ygd_prev)
        vyg = p.step("vector", lambda e: e.tensor_mul(ygb[:, :], ybuf[:, :],
                                                      szcb[:, :]), sDV)
        p.wait("scalar", sDV, vyg)
        ygd_prev = p.step("scalar", lambda e, cc=c: e.dma_start(
            out=dY.rearrange("(t p) l -> p t l", p=128)[:, :, cc * CS:(cc + 1) * CS],
            in_=ygb[:, :].rearrange("p (t l) -> p t l", l=CS)), sa, 16)
        vyg2 = p.step("vector", lambda e: e.tensor_mul(ybuf[:, :], ygb[:, :],
                                                       ygb[:, :]), sDV)
        p.waitt("tensor", r_cons[1])
        p.wait("tensor", sDV, vyg2)
        vss = None
        for i in range(32):
            fn = (lambda e, ii=i: e.matmul(
                S(3, 256, p=1), onecolB[:, :], ybuf[:, ii * CS:(ii + 1) * CS],
                start=(ii == 0), stop=(ii == 31)))
            if i == 31:
                vss = p.step("tensor", fn, sPE)
            else:
                p.step("tensor", fn)
        ss_prev = vss
        p.wait("scalar", sPE, vss)
        p.wait("scalar", sq, rsr_dma)
        p.step("scalar", lambda e: e.activation(
            rsr[0:1, :], S(3, 256, p=1), AF.Ln,
            bias=epscol[0:1, :], scale=1.0 / DS))
        vrs = p.step("scalar", lambda e: e.activation(
            rsr[0:1, :], rsr[0:1, :], AF.Exp, scale=-0.5), sAC)
        r_cons[1] = (sAC, vrs)
        p.wait("sync", sAC, vrs)
        rsr_dma = p.step("sync", lambda e, cc=c: e.dma_start(
            out=dRS[0:1, cc * CS:(cc + 1) * CS], in_=rsr[0:1, :]), sq, 16)
        vac = p.step("scalar", lambda e, cc=c: e.activation(
            dpc5[:, :], dAcFr[:, cc * CS + CS - 1:cc * CS + CS], AF.Exp), sAC)
        p.wait("vector", sAC, vac)
        vrh5 = p.step("vector", lambda e: e.tensor_scalar(
            rhsS5[:, :], cE64B5[:, :], dpc5[:, :], None, ALU.mult), sDV)
        for half in range(2):
            p.wait("tensor", sDV, sDV.n)
            p.wait("tensor", sAC, vrs)
            p.waitt("tensor", yo_cons)
            p.waitt("tensor", yd_cons[0])
            p.waitt("tensor", yd_cons[1])
            vmm = None
            for jj in range(4):
                fn = (lambda e, hh=half, j4=jj: e.matmul(
                    Sv(1024 + j4 * 512, 512), ones64B5[:, :],
                    rhsS5[:, hh * 2048 + j4 * 512: hh * 2048 + (j4 + 1) * 512],
                    start=True, stop=True))
                if jj == 3:
                    vmm = p.step("tensor", fn, sPE)
                else:
                    p.step("tensor", fn)
            p.wait("vector", sPE, vmm)
            p.wait("vector", sq, vst)
            p.step("vector", lambda e, hh=half: e.tensor_mul(
                carry2[:, hh * 2048:(hh + 1) * 2048],
                carry2[:, hh * 2048:(hh + 1) * 2048], Sv(1024, 2048)))
            vup = p.step("vector", lambda e, hh=half: e.tensor_add(
                carry2[:, hh * 2048:(hh + 1) * 2048],
                carry2[:, hh * 2048:(hh + 1) * 2048],
                ststg5[:, hh * 2048:(hh + 1) * 2048]), sDV)
            if half == 1:
                st_dv = vup
                yo_cons = (sDV, vup)
                yd_cons = [(sDV, vup), (sDV, vup)]
        cb5_pe = xs_pe

    p.wait("sync", sq, rsr_dma)
    p.step("sync", lambda e: e.dma_start(
        out=rstdT[:, :], in_=dRS.rearrange("o (t p) -> (o p) t", p=128),
        allow_slow_non_contiguous=True), sq, 16)
    p.run("ph5")
    g5.close()

    g6 = Guards(nc)
    wall = [g6.sb(f"wall{i}", (128, KDS * 512), BF16) for i in range(4)]
    yslab = [g6.sb(f"yslab{i}", (128, KDS * 128), BF16) for i in range(2)]
    xtile = [g6.sb(f"xtile{i}", (128, H), F32) for i in range(2)]
    gtile = [g6.sb("gtile0", (128, H), F32)]
    normwS = g6.sb("normwS", (128, KDS), F32)
    xnstg = g6.sb("xnstg", (128, 512), F32)
    stile = g6.sb("stile", (128, H), BF16)
    gsb6 = g6.sb("gsb6", (128, H), BF16)
    sgstg = g6.sb("sgstg", (128, H), F32)
    ostg6 = [g6.sb("ostg60", (128, H), F32)]

    p = Prog(nc, lastwait, selfsems, lastdone)
    barrier(p)
    p.step("sync", lambda e: e.dma_start(out=normwS[:, :], in_=normwD), sq, 16)
    nw_ld = sq.n
    w_ld = []
    for fc in range(4):
        v = p.step("gpsimd", lambda e, ff=fc: e.dma_start(
            out=wall[ff][:, :].rearrange("p (kc f) -> p kc f", f=512),
            in_=WoutT[:, fc * 512:(fc + 1) * 512].rearrange("(kc p) f -> p kc f", p=128)
            if False else
            WoutT[:, ff * 512:(ff + 1) * 512].rearrange("(kc p) f -> p kc f", p=128)),
            sg, 16)
        w_ld.append(v)
    p.wait("vector", sq, nw_ld)
    vnw = None
    for fc in range(4):
        p.wait("vector", sg, w_ld[fc])
        for kc in range(KDS):
            fn = (lambda e, ff=fc, k=kc: e.tensor_scalar(
                wall[ff][:, k * 512:(k + 1) * 512], wall[ff][:, k * 512:(k + 1) * 512],
                normwS[:, k:k + 1], None, ALU.mult))
            if fc == 3 and kc == KDS - 1:
                vnw = p.step("vector", fn, sDV)
            else:
                p.step("vector", fn)

    yfree = [0, 0]
    xfree6 = [0, 0]
    gfree = [0]
    ofree = [0]
    oslot = [None, None]
    for tt in range(16):
        b = tt % 2
        p.wait("sync", sPE, yfree[b])
        vy = p.step("sync", lambda e, t=tt, bb=b: e.dma_start(
            out=yslab[bb][:, :].rearrange("p (kc t) -> p kc t", t=128),
            in_=dY.rearrange("(kc p) t -> p kc t", p=128)[:, :, t * 128:(t + 1) * 128]),
            sq, 16)
        p.wait("sync", sDV, xfree6[b])
        vx = p.step("sync", lambda e, t=tt, bb=b: e.dma_start(
            out=xtile[bb][:, :], in_=xin[HALO + t * 128: HALO + (t + 1) * 128, :]),
            sq, 16)
        p.wait("sync", sAC, gfree[0])
        vg = p.step("sync", lambda e, t=tt: e.dma_start(
            out=gtile[0][:, :], in_=dG[t * 128:(t + 1) * 128, :]), sq, 16)
        vs_dv = None
        for fc in range(4):
            sl = fc % 2
            p.wait("tensor", sq, vy)
            p.wait("tensor", sDV, vnw)
            p.waitt("tensor", oslot[sl])
            vO = None
            for kc in range(KDS):
                fn = (lambda e, bb=b, k=kc, ff=fc, ss=sl: e.matmul(
                    S(ss), yslab[bb][:, k * 128:(k + 1) * 128],
                    wall[ff][:, k * 512:(k + 1) * 512],
                    start=(k == 0), stop=(k == KDS - 1)))
                if kc == KDS - 1:
                    vO = p.step("tensor", fn, sPE)
                else:
                    p.step("tensor", fn)
            if fc == 3:
                yfree[b] = vO
            p.wait("vector", sq, vx)
            p.step("vector", lambda e, bb=b, ff=fc, t=tt: e.tensor_scalar(
                xnstg[:, :], xtile[bb][:, ff * 512:(ff + 1) * 512],
                lnm[:, t:t + 1], lnr[:, t:t + 1], ALU.add, ALU.mult))
            p.wait("vector", sPE, vO)
            vs_dv = p.step("vector", lambda e, ff=fc, ss=sl, t=tt:
                           e.scalar_tensor_tensor(
                stile[:, ff * 512:(ff + 1) * 512], S(ss), rstdT[:, t:t + 1],
                xnstg[:, :], ALU.mult, ALU.add), sDV)
            oslot[sl] = (sDV, vs_dv)
        p.wait("scalar", sq, vg)
        vgs = p.step("scalar", lambda e: e.activation(
            gsb6[:, :], gtile[0][:, :], AF.Silu), sAC)
        gfree[0] = vgs
        p.wait("vector", sAC, vgs)
        p.step("vector", lambda e: e.tensor_mul(sgstg[:, :], stile[:, :], gsb6[:, :]))
        p.wait("vector", sOut, ofree[0])
        vout = p.step("vector", lambda e, bb=b: e.tensor_add(
            ostg6[0][:, :], xtile[bb][:, :], sgstg[:, :]), sDV)
        xfree6[b] = vout
        p.wait("sync", sDV, vout)
        ofree[0] = p.step("sync", lambda e, t=tt: e.dma_start(
            out=out[t * 128:(t + 1) * 128, :], in_=ostg6[0][:, :]), sOut, 16)
    p.wait("sync", sOut, sOut.n)
    p.run("ph6")
    g6.close()

    es.close()
    return nc


_NC_CACHE = {}


def _get_nc(dbg=False):
    key = bool(dbg)
    if key not in _NC_CACHE:
        _NC_CACHE[key] = build_nc(dbg)
    return _NC_CACHE[key]


def _consts():
    c = {}
    c["cIdent"] = np.eye(128, dtype=np.float32)
    s2 = np.zeros((2, 128), np.float32)
    s2[0, :64] = 1.0
    s2[1, 64:] = 1.0
    c["cSel2F"] = s2
    s16 = np.zeros((16, 16 * 128), np.float32)
    for e in range(16):
        s16[e, e * 128:(e + 1) * 128] = 1.0
    c["cSel16F"] = s16
    sp = np.zeros((64, 32 * 128), np.float32)
    for i in range(32):
        sp[2 * i, i * 128:i * 128 + 64] = 1.0
        sp[2 * i + 1, i * 128 + 64:(i + 1) * 128] = 1.0
    c["cSelPair"] = sp
    e64 = np.zeros((64, DS), np.float32)
    for h in range(64):
        e64[h, h * 64:(h + 1) * 64] = 1.0
    c["cE64"] = e64
    c["cOnes64F"] = np.ones((64, 128), np.float32)
    c["cOnesCol"] = np.ones((128, 1), np.float32)
    tr = np.zeros((128, 384), np.float32)
    sidx = np.arange(128)[:, None]
    tr[:, 0:256] = (np.arange(256)[None, :] >= sidx).astype(np.float32)
    tr[:, 256:384] = (np.arange(128)[None, :] >= sidx).astype(np.float32)
    c["cTril"] = tr
    return c


def kernel(L0_attn_x, L0_attn_rms_w, L0_attn_W_in, L0_attn_conv_w, L0_attn_conv_b,
           L0_attn_dt_bias, L0_attn_A_log, L0_attn_D, L0_attn_norm_w, L0_attn_W_out,
           L0_ffn_gate_w, L0_ffn_w13, L0_ffn_w2, L0_ffn_sw13, L0_ffn_sw2):
    f32 = np.float32
    x = np.asarray(L0_attn_x, f32)
    B, L, _ = x.shape

    WinT = np.ascontiguousarray(np.asarray(L0_attn_W_in, f32).T)
    WoutT = np.ascontiguousarray(np.asarray(L0_attn_W_out, f32).T)
    gateT = np.ascontiguousarray(np.asarray(L0_ffn_gate_w, f32).T)
    w13T = np.ascontiguousarray(
        np.asarray(L0_ffn_w13, f32).transpose(0, 2, 1)).reshape(NE * H, 2 * FI)
    w2T = np.ascontiguousarray(
        np.asarray(L0_ffn_w2, f32).transpose(0, 2, 1)).reshape(NE * FI, H)
    sw13T = np.ascontiguousarray(np.asarray(L0_ffn_sw13, f32).T)
    sw2T = np.ascontiguousarray(np.asarray(L0_ffn_sw2, f32).T)
    convw = np.ascontiguousarray(
        np.asarray(L0_attn_conv_w, f32).reshape(34, 128, 4).transpose(1, 0, 2)
        .reshape(128, 34 * 4))
    convb = np.ascontiguousarray(
        np.asarray(L0_attn_conv_b, f32).reshape(34, 128).T)
    dtb = np.asarray(L0_attn_dt_bias, f32).reshape(NH, 1)
    alog = np.asarray(L0_attn_A_log, f32).reshape(NH, 1)
    dpair = np.ascontiguousarray(np.asarray(L0_attn_D, f32).reshape(32, 2).T)
    rmsw = np.ascontiguousarray(np.asarray(L0_attn_rms_w, f32).reshape(KH, 128).T)
    normw = np.ascontiguousarray(np.asarray(L0_attn_norm_w, f32).reshape(KDS, 128).T)
    cst = _consts()

    nc = _get_nc(False)
    in_maps = []
    for k in range(NCORE):
        b = k // 4
        q = k % 4
        t0 = q * TPC
        xin = np.zeros((TH, H), f32)
        if t0 >= HALO:
            xin[0:HALO] = x[b, t0 - HALO:t0]
        xin[HALO:] = x[b, t0:t0 + TPC]
        rank = q
        maskB = np.zeros((128, 4), f32)
        maskQ = np.zeros((128, 4), f32)
        maskQ[:, rank] = 1.0
        tsel = np.zeros((64, 4), f32)
        for j in range(4):
            if j < rank:
                maskB[:, j] = 1.0
            if 1 <= j < rank:
                tsel[:, j] = 1.0
        m = dict(
            xin=np.ascontiguousarray(xin), WinT=WinT, WoutT=WoutT, gateT=gateT,
            w13T=w13T, w2T=w2T, sw13T=sw13T, sw2T=sw2T, convw=convw, convb=convb,
            dtb=dtb, alog=alog, dpair=dpair, rmsw=rmsw, normw=normw,
            cMaskB=maskB, cMaskQ=maskQ,
            cTsel=tsel, cTselC=np.ascontiguousarray(1.0 - tsel),
            **cst)
        in_maps.append(m)

    res = run_bass_kernel_spmd(nc, in_maps, core_ids=list(range(NCORE)))
    outp = np.empty((B, L, H), f32)
    for k in range(NCORE):
        b = k // 4
        q = k % 4
        outp[b, q * TPC:(q + 1) * TPC, :] = res.results[k]["out"]
    return outp
